# version 01
# baseline (speedup 1.0000x reference)
import math
import numpy as np
import ml_dtypes
import concourse.bass as bass
import concourse.mybir as mybir
from concourse.bass_utils import run_bass_kernel_spmd

F32 = mybir.dt.float32
BF16 = mybir.dt.bfloat16
AF = mybir.ActivationFunctionType
ALU = mybir.AluOpType
AX = mybir.AxisListType

NCORES = 8
D = 4096
SEQ = 8192
PER = SEQ // NCORES
NT = PER // 128
NH = 32
HD = 128
INTER = 11008
SCALE = HD ** -0.5
EPS = 1e-5
NKC = SEQ // HD
NKCL = NKC + 8
NQB = PER // 512
NW = NKCL // 2
IB_SIZES = [512] * 21 + [256]
BF = ml_dtypes.bfloat16

NSUB = 4
DUMP = False


def _slopes():
    base = 2.0 ** (-(2.0 ** -(math.log2(NH) - 3)))
    return [base ** (h + 1) for h in range(NH)]


def _split3(x):
    hi = x.astype(BF).astype(np.float64)
    mid = (x - hi).astype(BF).astype(np.float64)
    lo = (x - hi - mid).astype(BF).astype(np.float64)
    return hi, mid, lo


def _split2(x):
    hi = x.astype(BF).astype(np.float64)
    lo = (x - hi).astype(BF).astype(np.float64)
    return hi, lo


def build_tables(core):
    slopes = _slopes()
    q_glob0 = core * PER
    k0_of_kc = [q_glob0 + kc * HD for kc in range(8)] + \
               [(kc - 8) * HD for kc in range(8, NKCL)]

    lpack = np.zeros((NQB, 128, NKCL * 128), dtype=BF)
    for qb in range(NQB):
        q0 = q_glob0 + qb * 512
        for kc in range(NKCL):
            k0 = k0_of_kc[kc]
            dup = kc >= 8 and (kc - 8) // 8 == core
            if dup or k0 >= q0 + 512:
                rows = np.array([-1e10, 0, 0, -1e10, 0, 0, 0, 0], np.float64)
            else:
                sp = _split3(np.array([k0 / SCALE], np.float64))
                rows = np.array([sp[0][0], sp[1][0], sp[2][0],
                                 sp[0][0], sp[1][0], 1, 1, 1], np.float64)
            tile = np.repeat(rows[:, None], 128, axis=1).astype(BF)
            for b in range(4):
                lpack[qb, 32 * b:32 * b + 8, kc * 128:(kc + 1) * 128] = tile

    rpack = np.zeros((NQB, 128, 8 * 512), dtype=BF)
    for qb in range(NQB):
        q0 = q_glob0 + qb * 512
        f = np.arange(512, dtype=np.float64)
        for h in range(NH):
            sp = _split2(np.array([slopes[h]], np.float64))
            s_hi, s_lo = float(sp[0][0]), float(sp[1][0])
            v_hi, v_mid, v_lo = _split3(-slopes[h] * (q0 + f) / SCALE)
            rows = np.stack([np.full(512, s_hi), np.full(512, s_hi),
                             np.full(512, s_hi), np.full(512, s_lo),
                             np.full(512, s_lo), v_hi, v_mid, v_lo])
            pb, fo = 32 * (h % 4), 512 * (h // 4)
            rpack[qb, pb:pb + 8, fo:fo + 512] = rows.astype(BF)

    expb = np.empty((128, NH), dtype=np.float32)
    for h in range(NH):
        expb[:, h] = np.float32(slopes[h]) * np.arange(128, dtype=np.float32)

    mask = np.zeros((128, 2048), dtype=np.float32)
    p = np.arange(128)[:, None]
    f = np.arange(512)[None, :]
    for j in range(4):
        mask[:, j * 512:(j + 1) * 512] = np.where(f < 128 * j + p, -1e9, 0.0)
    return dict(lpack=lpack, rpack=rpack, expb=expb, mask=mask.astype(BF))


class Arena:
    def __init__(self, tensor, total):
        self.t = tensor
        self.total = total

    def bf(self, off, elems):
        assert off + elems <= self.total
        return self.t[:, off:off + elems]

    def f32(self, off, elems):
        assert off % 2 == 0 and off + 2 * elems <= self.total
        return self.t[:, off:off + 2 * elems].bitcast(F32)


class P:
    def __init__(self):
        self.h = {}
        self.n = {}

    def sem(self, name, handle):
        self.h[name] = handle
        self.n[name] = 0

    def inc(self, name, instr, k=1):
        instr.then_inc(self.h[name], k)
        self.n[name] += k
        return self.n[name]

    def w(self, eng, name, val):
        if val > 0:
            eng.wait_ge(self.h[name], val)


class Chain:
    OPS = {"tensor_tensor", "tensor_copy", "tensor_add", "tensor_sub",
           "tensor_mul", "tensor_max", "tensor_scalar", "tensor_scalar_mul",
           "tensor_scalar_add", "tensor_single_scalar", "scalar_tensor_tensor",
           "reduce_sum", "reduce_max", "tensor_reduce", "memset", "memzero",
           "reciprocal", "activation", "copy", "add", "mul", "square",
           "transpose", "select"}

    def __init__(self, E, eng, sem):
        self._E = E
        self._e = eng
        self._s = sem

    def __getattr__(self, name):
        attr = getattr(self._e, name)
        if name not in Chain.OPS:
            return attr

        def wrapped(*a, **kw):
            E = self._E
            E.w(self._e, self._s, E.n[self._s])
            instr = attr(*a, **kw)
            return E.inc(self._s, instr)
        return wrapped


SEMS = ["ld", "lg", "ini", "pre", "img", "imt", "imu", "imga", "imgb",
        "bar", "lt", "lx", "kv", "kva", "kvb", "kg", "cc", "qpa", "qpb",
        "qga", "qgb", "qe", "qa", "qb", "rs", "sp", "pv", "tl", "od",
        "wda", "wdb", "woa", "wob", "wm", "fxa", "fxb", "fwa", "fwb",
        "fua", "fub", "dw", "fd", "st", "vch", "ach", "oda", "odb"]


def build_nc(n_sub=NSUB, dump=DUMP):
    nc = bass.Bass("TRN2", target_bir_lowering=False, debug=False)
    E = P()

    x_in = nc.dram_tensor("x", [PER, D], F32, kind="ExternalInput").ap()
    out = nc.dram_tensor("out", [PER, D], F32, kind="ExternalOutput").ap()
    W = {}
    for nm, shp in (("wq_a", [D, D]), ("wkv_a", [D, 256]), ("wo_a", [D, D]),
                    ("wq_b", [D, D]), ("wkv_b", [D, 256]), ("wo_b", [D, D]),
                    ("gate_w", [INTER, D]), ("up_w", [INTER, D]),
                    ("down_w", [D, INTER])):
        W[nm] = nc.dram_tensor(nm, shp, F32, kind="ExternalInput").ap()
    gs_b = nc.dram_tensor("gs_b", [128, 1], F32, kind="ExternalInput").ap()
    us_b = nc.dram_tensor("us_b", [128, 1], F32, kind="ExternalInput").ap()
    ds_b = nc.dram_tensor("ds_b", [128, 1], F32, kind="ExternalInput").ap()
    t_lpack = nc.dram_tensor("t_lpack", [NQB, 128, NKCL * 128], BF16, kind="ExternalInput").ap()
    t_rpack = nc.dram_tensor("t_rpack", [NQB, 128, 8 * 512], BF16, kind="ExternalInput").ap()
    t_expb = nc.dram_tensor("t_expb", [128, NH], F32, kind="ExternalInput").ap()
    t_mask = nc.dram_tensor("t_mask", [128, 2048], BF16, kind="ExternalInput").ap()
    ident_in = nc.dram_tensor("ident", [128, 128], F32, kind="ExternalInput").ap()

    gT = nc.dram_tensor("gT", [D, INTER], BF16)
    uT = nc.dram_tensor("uT", [D, INTER], BF16)
    dT = nc.dram_tensor("dT", [INTER, D], BF16)
    g_nat = nc.dram_tensor("g_nat", [INTER, D], BF16)
    u_nat = nc.dram_tensor("u_nat", [INTER, D], BF16)
    d_nat = nc.dram_tensor("d_nat", [D, INTER], BF16)
    wo_bf = nc.dram_tensor("wo_bf", [D, D], BF16)
    xt2 = nc.dram_tensor("xt2", [32, 128, PER], BF16)
    qt_c = nc.dram_tensor("qt_c", [NH, 128, PER], BF16)
    o_c = nc.dram_tensor("o_c", [NH, 128, 512], BF16)
    kt_own = nc.dram_tensor("kt_own", [128, PER], BF16)
    v_own = nc.dram_tensor("v_own", [PER, HD], BF16)
    kt_g = nc.dram_tensor("kt_g", [NCORES * 128, PER], BF16)
    v_g = nc.dram_tensor("v_g", [SEQ, HD], BF16)

    ARENA = 37888

    import contextlib
    with contextlib.ExitStack() as STK:
        res = STK.enter_context(nc.sbuf_tensor("res", [128, NT, D], F32))
        ident_f = STK.enter_context(nc.sbuf_tensor("ident_f", [128, 128], F32))
        ones_f = STK.enter_context(nc.sbuf_tensor("ones_f", [128, 1], F32))
        ones1_f = STK.enter_context(nc.sbuf_tensor("ones1_f", [1, 128], F32))
        gsb = STK.enter_context(nc.sbuf_tensor("gsb", [128, 1], F32))
        usb = STK.enter_context(nc.sbuf_tensor("usb", [128, 1], F32))
        dsb = STK.enter_context(nc.sbuf_tensor("dsb", [128, 1], F32))
        expb_t = STK.enter_context(nc.sbuf_tensor("expb_t", [128, NH], F32))
        mask_b = STK.enter_context(nc.sbuf_tensor("mask_b", [128, 2048], BF16))
        epsb = STK.enter_context(nc.sbuf_tensor("epsb", [128, 1], F32))
        ln_st = STK.enter_context(nc.sbuf_tensor("ln_st", [128, 8], F32))
        ar_t = STK.enter_context(nc.sbuf_tensor("arena", [128, ARENA], BF16))
        pool = STK.enter_context(nc.psum_tensor("pool", [128, 4096], F32))
        for nm in SEMS:
            E.sem(nm, STK.enter_context(nc.semaphore("s_" + nm)))
        block = STK.enter_context(nc.Block())
        A = Arena(ar_t, ARENA)

        def BV(f):
            block.vector(lambda v: f(Chain(E, v, "vch")))

        def BS(f):
            block.scalar(lambda sc: f(Chain(E, sc, "ach")))

        @block.sync
        def _(s):
            E.inc("ld", s.dma_start(out=res[:, :, :],
                                    in_=x_in.rearrange("(t p) d -> p t d", p=128)), 16)
            E.inc("ld", s.dma_start(out=mask_b[:, :], in_=t_mask), 16)
            E.inc("ld", s.dma_start(out=expb_t[:, :], in_=t_expb), 16)

        @block.gpsimd
        def _(g):
            E.inc("lg", g.dma_start(out=ident_f[:, :], in_=ident_in), 16)
            E.inc("lg", g.dma_start(out=gsb[:, :], in_=gs_b), 16)
            E.inc("lg", g.dma_start(out=usb[:, :], in_=us_b), 16)
            E.inc("lg", g.dma_start(out=dsb[:, :], in_=ds_b), 16)

        def _init(v):
            v.memset(ones_f[:, :], 1.0)
            v.memset(epsb[:, :], float(EPS))
            v.memset(ones1_f[:, :], 1.0)
        BV(_init)
        n_ld, n_lg, n_ini = E.n["ld"], E.n["lg"], E.n["vch"]

        @block.gpsimd
        def _(g):
            for (src, dst, rows) in ((W["gate_w"], g_nat, INTER),
                                     (W["up_w"], u_nat, INTER),
                                     (W["down_w"], d_nat, D)):
                step = rows // 8
                for c in range(8):
                    E.inc("pre", g.dma_start(out=dst.ap()[c * step:(c + 1) * step, :],
                                             in_=src[c * step:(c + 1) * step, :]), 16)
        n_pre_cast = E.n["pre"]

        BAR_SEMS = ["lt", "sp", "pv", "tl", "wm", "fd", "vch", "ach",
                    "ld", "lg", "lx", "kv", "kva", "kvb", "kg", "qpa", "qpb",
                    "qga", "qgb", "qe", "qa", "qb", "rs", "oda", "odb", "wda", "wdb",
                    "woa", "wob", "fxa", "fxb", "fwa", "fwb", "fua", "fub",
                    "dw", "pre", "imt", "imu", "imga", "imgb", "cc"]

        def barrier():
            vals = [(nm, E.n[nm]) for nm in BAR_SEMS if E.n[nm] > 0]
            for eng in (block.tensor, block.vector, block.scalar, block.sync,
                        block.gpsimd):
                @eng
                def _(e):
                    for nm, val in vals:
                        e.wait_ge(E.h[nm], val)

        def emit_ln():
            ln_sq = A.bf(0, 4096)
            ln_xb = A.f32(4096, 4096)
            ln_ts = A.bf(12288, 4096)
            last_ident = [0]
            last_store = [0]
            last_tp = [0]
            for t in range(NT):
                def f(v, t=t):
                    if t == 0:
                        E.w(v._e, "ld", n_ld)
                        E.w(v._e, "lg", n_lg)
                    else:
                        E.w(v._e, "ach", last_ident[0])
                    v.reduce_sum(ln_st[:, 0:1], res[:, t, :], axis=AX.X)
                BV(f)
                m_v1 = E.n["vch"]

                def f(sc, t=t, m_v1=m_v1):
                    E.w(sc._e, "vch", m_v1)
                    sc.activation(ln_sq, res[:, t, :], AF.Square,
                                  accum_out=ln_st[:, 1:2])
                BS(f)
                m_a1 = E.n["ach"]

                def f(v, m_a1=m_a1):
                    E.w(v._e, "ach", m_a1)
                    v.tensor_scalar_mul(ln_st[:, 2:3], ln_st[:, 0:1], 1.0 / D)
                    v.tensor_scalar_mul(ln_st[:, 3:4], ln_st[:, 1:2], 1.0 / D)
                    v.tensor_mul(ln_st[:, 4:5], ln_st[:, 2:3], ln_st[:, 2:3])
                    v.tensor_sub(ln_st[:, 5:6], ln_st[:, 3:4], ln_st[:, 4:5])
                BV(f)
                m_v2 = E.n["vch"]

                def f(sc, m_v2=m_v2):
                    E.w(sc._e, "vch", m_v2)
                    sc.activation(ln_st[:, 6:7], ln_st[:, 5:6], AF.Ln, bias=epsb[:, :])
                    sc.activation(ln_st[:, 6:7], ln_st[:, 6:7], AF.Exp, scale=-0.5)
                BS(f)
                m_a2 = E.n["ach"]

                def f(v, m_a2=m_a2):
                    E.w(v._e, "ach", m_a2)
                    v.tensor_mul(ln_st[:, 7:8], ln_st[:, 2:3], ln_st[:, 6:7])
                    v.tensor_scalar_mul(ln_st[:, 7:8], ln_st[:, 7:8], -1.0)
                BV(f)
                m_v3 = E.n["vch"]

                def f(sc, t=t, m_v3=m_v3):
                    E.w(sc._e, "vch", m_v3)
                    E.w(sc._e, "lt", last_tp[0])
                    sc.activation(ln_xb, res[:, t, :], AF.Identity,
                                  scale=ln_st[:, 6:7], bias=ln_st[:, 7:8])
                BS(f)
                last_ident[0] = E.n["ach"]
                m_a3 = E.n["ach"]

                tms = []
                cms = []
                for gb in range(4):
                    @block.tensor
                    def _(te, gb=gb, m_a3=m_a3):
                        E.w(te, "ach", m_a3)
                        if gb >= 2:
                            E.w(te, "vch", cms[gb - 2])
                        sl = 2048 + (gb % 2) * 1024
                        last = None
                        for b in range(8):
                            hb = gb * 8 + b
                            last = te.transpose(pool[:, sl + b * 128:sl + (b + 1) * 128],
                                                ln_xb[:, hb * 128:(hb + 1) * 128],
                                                ident_f[:, :])
                        tms.append(E.inc("lt", last))

                    def f(v, gb=gb):
                        E.w(v._e, "lt", tms[gb])
                        if gb == 0:
                            E.w(v._e, "lx", last_store[0])
                        sl = 2048 + (gb % 2) * 1024
                        v.tensor_copy(ln_ts[:, gb * 1024:(gb + 1) * 1024],
                                      pool[:, sl:sl + 1024])
                        cms.append(E.n["vch"])
                    BV(f)
                last_tp[0] = tms[3]

                @block.sync
                def _(s, t=t):
                    E.w(s, "vch", cms[3])
                    E.inc("lx", s.dma_start(
                        out=xt2.ap()[:, :, t * 128:(t + 1) * 128]
                            .rearrange("hb p j -> p hb j"),
                        in_=ln_ts.rearrange("p (hb j) -> p hb j", hb=32)), 16)
                last_store[0] = E.n["lx"]

        def emit_kv(wkv, KT_sb, V_sb):
            kv_w = A.bf(18432, 8192).rearrange("p (hb e) -> p hb e", hb=32)
            kv_x0 = A.bf(27648, 2048).rearrange("p (sl s) -> p sl s", sl=2)
            kv_o = A.bf(29696, 2048)
            ps_k = pool[:, 0:1024]
            ps_v = pool[:, 1024:2048]

            @block.gpsimd
            def _(g):
                E.w(g, "lx", E.n["lx"])
                E.inc("kg", g.dma_start(
                    out=kv_w, in_=wkv.rearrange("(hb p) e -> p hb e", p=128)), 16)
            m_w = E.n["kg"]

            xm, mmk = [], []
            for hb in range(32):
                @block.sync
                def _(s, hb=hb):
                    if hb >= 2:
                        E.w(s, "sp", mmk[hb - 2])
                    else:
                        E.w(s, "lx", E.n["lx"])
                        E.w(s, "sp", E.n["sp"])
                    nm = "kva" if hb % 2 == 0 else "kvb"
                    xm.append((nm, E.inc(nm, s.dma_start(
                        out=kv_x0[:, hb % 2, :], in_=xt2.ap()[hb, :, :]), 16)))

                @block.tensor
                def _(te, hb=hb):
                    E.w(te, xm[hb][0], xm[hb][1])
                    if hb == 0:
                        E.w(te, "kg", m_w)
                        E.w(te, "vch", E.n["vch"])
                    sl = hb % 2
                    st0, sp0 = (hb == 0), (hb == 31)
                    for half in range(2):
                        te.matmul(ps_k[:, half * 512:(half + 1) * 512],
                                  kv_w[:, hb, 0:128],
                                  kv_x0[:, sl, half * 512:half * 512 + 512],
                                  start=st0, stop=sp0)
                    last = None
                    for half in range(2):
                        last = te.matmul(ps_v[:, half * 512:(half + 1) * 512],
                                         kv_w[:, hb, 128:256],
                                         kv_x0[:, sl, half * 512:half * 512 + 512],
                                         start=st0, stop=sp0)
                    mmk.append(E.inc("sp", last))

            vt_f = A.f32(27648, 1024)

            def f(v):
                E.w(v._e, "sp", mmk[31])
                v.tensor_copy(kv_o[:, 0:1024], ps_k)
                v.tensor_copy(vt_f, ps_v)
            BV(f)
            m_vt0 = E.n["vch"]

            @block.tensor
            def _(te):
                E.w(te, "vch", m_vt0)
                last = None
                for st in range(8):
                    last = te.transpose(pool[:, 2048 + st * 128:2048 + (st + 1) * 128],
                                        vt_f[:, st * 128:(st + 1) * 128],
                                        ident_f[:, :])
                E.inc("lt", last)
            m_vtp = E.n["lt"]

            def f(v):
                E.w(v._e, "lt", m_vtp)
                v.tensor_copy(kv_o[:, 1024:2048], pool[:, 2048:3072])
            BV(f)
            m_ev = E.n["vch"]

            @block.sync
            def _(s):
                E.w(s, "vch", m_ev)
                E.inc("kv", s.dma_start(out=kt_own.ap(), in_=kv_o[:, 0:1024]), 16)
                E.inc("kv", s.dma_start(
                    out=v_own.ap().rearrange("(st p) d -> p st d", p=128),
                    in_=kv_o[:, 1024:2048].rearrange("p (st d) -> p st d", st=8)), 16)
            m_own = E.n["kv"]

            @block.gpsimd
            def _(g):
                E.w(g, "kv", m_own)
                E.inc("cc", g.collective_compute(
                    "AllGather", ALU.bypass, replica_groups=[list(range(NCORES))],
                    ins=[kt_own.ap().opt()], outs=[kt_g.ap().opt()]))
                E.inc("cc", g.collective_compute(
                    "AllGather", ALU.bypass, replica_groups=[list(range(NCORES))],
                    ins=[v_own.ap().opt()], outs=[v_g.ap().opt()]))
            m_cc = E.n["cc"]

            @block.sync
            def _(s):
                E.w(s, "kv", m_own)
                E.inc("kv", s.dma_start(out=KT_sb[:, 0:1024], in_=kt_own.ap()), 16)
                E.inc("kv", s.dma_start(
                    out=V_sb[:, 0:1024].rearrange("p (kb j) -> p kb j", kb=8),
                    in_=v_own.ap().rearrange("(kb p) j -> p kb j", p=128)), 16)
                E.w(s, "cc", m_cc)
                E.inc("kv", s.dma_start(
                    out=KT_sb[:, 1024:9216].rearrange("p (r s) -> p r s", r=8),
                    in_=kt_g.ap().rearrange("(r p) s -> p r s", p=128)), 16)
                E.inc("kv", s.dma_start(
                    out=V_sb[:, 1024:9216].rearrange("p (kb j) -> p kb j", kb=64),
                    in_=v_g.ap().rearrange("(kb p) j -> p kb j", p=128)), 16)
            return E.n["kv"]

        def emit_qproj(wq, m_kv):
            qp_x = A.bf(31744, 1024).rearrange("p (sl s) -> p sl s", sl=2)
            qp_w = A.bf(32768, 2048).rearrange("p (sl s) -> p sl s", sl=2)
            qp_o = A.bf(18432, 4096).rearrange("p (hh s) -> p hh s", hh=8)
            last_evac = [0]
            last_store = [0]
            for qh in range(2):
                for hg in range(4):
                    xm, wm, mm = [], [], []
                    for hb in range(32):
                        @block.sync
                        def _(s, qh=qh, hb=hb):
                            if hb >= 2:
                                E.w(s, "sp", mm[hb - 2])
                            else:
                                E.w(s, "kv", m_kv)
                                E.w(s, "sp", E.n["sp"])
                            nm = "qpa" if hb % 2 == 0 else "qpb"
                            xm.append((nm, E.inc(nm, s.dma_start(
                                out=qp_x[:, hb % 2, :],
                                in_=xt2.ap()[hb, :, qh * 512:(qh + 1) * 512]), 16)))

                        @block.gpsimd
                        def _(g, hg=hg, hb=hb):
                            if hb >= 2:
                                E.w(g, "sp", mm[hb - 2])
                            else:
                                E.w(g, "kv", m_kv)
                                E.w(g, "sp", E.n["sp"])
                            nm = "qga" if hb % 2 == 0 else "qgb"
                            wm.append((nm, E.inc(nm, g.dma_start(
                                out=qp_w[:, hb % 2, :],
                                in_=wq[hb * 128:(hb + 1) * 128,
                                       hg * 1024:(hg + 1) * 1024]), 16)))

                        @block.tensor
                        def _(te, hb=hb):
                            E.w(te, xm[hb][0], xm[hb][1])
                            E.w(te, wm[hb][0], wm[hb][1])
                            if hb == 0:
                                E.w(te, "vch", last_evac[0])
                            last = None
                            for hh in range(8):
                                last = te.matmul(pool[:, hh * 512:(hh + 1) * 512],
                                                 qp_w[:, hb % 2, hh * 128:(hh + 1) * 128],
                                                 qp_x[:, hb % 2, :],
                                                 start=(hb == 0), stop=(hb == 31))
                            mm.append(E.inc("sp", last))

                    def f(v, mm=mm):
                        E.w(v._e, "sp", mm[31])
                        E.w(v._e, "qe", last_store[0])
                        for hh in range(8):
                            v.tensor_copy(qp_o[:, hh, :],
                                          pool[:, hh * 512:(hh + 1) * 512])
                    BV(f)
                    last_evac[0] = E.n["vch"]
                    m_e = E.n["vch"]

                    @block.sync
                    def _(s, qh=qh, hg=hg, m_e=m_e):
                        E.w(s, "vch", m_e)
                        E.inc("qe", s.dma_start(
                            out=qt_c.ap()[hg * 8:(hg + 1) * 8, :,
                                          qh * 512:(qh + 1) * 512]
                                .rearrange("h p j -> p h j"),
                            in_=qp_o[:, :, :]), 16)
                    last_store[0] = E.n["qe"]

        def emit_attn_main(KT_sb, V_sb, lp_sb, rp_sb, wo_fp32, m_kv):
            at_qt = A.bf(31744, 1024).rearrange("p (sl s) -> p sl s", sl=2)
            at_pt = A.bf(32768, 2048).rearrange("p (sl s) -> p sl s", sl=2)
            at_acc = A.f32(34816, 512)
            at_rb = A.f32(35840, 512)
            lnl = at_rb[0:1, :]
            at_ot = A.bf(36864, 1024).rearrange("p (sl s) -> p sl s", sl=2)
            at_wt = at_qt
            at_oc = at_ot
            ps_s = pool[:, 0:2048]
            ps_o = pool[:, 2048:3072]
            ps_m = pool[:, 3072:3584]

            @block.gpsimd
            def _(g):
                for c in range(8):
                    E.inc("pre", g.dma_start(out=wo_bf.ap()[c * 512:(c + 1) * 512, :],
                                             in_=wo_fp32[c * 512:(c + 1) * 512, :]), 16)
            m_wo = E.n["pre"]

            prev_wo_mm = [0]
            prev_wo_ev = [0]
            for qb in range(NQB):
                @block.sync
                def _(s, qb=qb):
                    if qb == 0:
                        E.w(s, "kv", m_kv)
                    else:
                        E.w(s, "sp", E.n["sp"])
                    E.inc("rs", s.dma_start(out=lp_sb, in_=t_lpack[qb]), 16)
                    E.inc("rs", s.dma_start(out=rp_sb, in_=t_rpack[qb]), 16)
                m_tab = E.n["rs"]

                exm, pvm, lam, oem, odm, spm = {}, {}, {}, {}, {}, {}
                last_m1 = [E.n["tl"]]

                for h in range(NH):
                    par = h % 2

                    q_nm = "qa" if par == 0 else "qb"

                    @block.sync
                    def _(s, qb=qb, h=h, par=par, q_nm=q_nm):
                        if h >= 2:
                            E.w(s, "sp", spm[(h - 2, NW - 1)])
                        elif qb > 0:
                            E.w(s, "sp", E.n["sp"])
                        if qb > 0:
                            E.w(s, "wm", prev_wo_mm[0])
                        E.inc(q_nm, s.dma_start(
                            out=at_qt[:, par, :],
                            in_=qt_c.ap()[h, :, qb * 512:(qb + 1) * 512]), 16)
                    m_q = E.n[q_nm]

                    for w in range(NW):
                        slot = w % 2
                        diag = w in (2 * qb, 2 * qb + 1)

                        @block.tensor
                        def _(te, qb=qb, h=h, w=w, slot=slot, par=par, m_q=m_q,
                              q_nm=q_nm):
                            if w == 0:
                                E.w(te, q_nm, m_q)
                                if h == 0:
                                    E.w(te, "rs", m_tab)
                                    E.w(te, "qe", E.n["qe"])
                                    if qb > 0:
                                        E.w(te, "vch", prev_wo_ev[0])
                            key = (h, w - 2) if w >= 2 else (h - 1, NW + w - 2)
                            if key in exm:
                                E.w(te, "ach", exm[key])
                            hb4 = 32 * (h % 4)
                            last = None
                            for j in range(2):
                                kc = 2 * w + j
                                dst = ps_s[:, slot * 1024 + j * 512:
                                           slot * 1024 + (j + 1) * 512]
                                te.matmul(dst, KT_sb[:, kc * 128:(kc + 1) * 128],
                                          at_qt[:, par, :], start=True, stop=False)
                                last = te.matmul(
                                    dst, lp_sb[hb4:hb4 + 8, kc * 128:(kc + 1) * 128],
                                    rp_sb[hb4:hb4 + 8,
                                          512 * (h // 4):512 * (h // 4) + 512],
                                    start=False, stop=True, tile_position=(hb4, 0))
                            spm[(h, w)] = E.inc("sp", last)

                        mkv = None
                        if diag:
                            def f(v, qb=qb, h=h, w=w, slot=slot):
                                E.w(v._e, "sp", spm[(h, w)])
                                wj = w - 2 * qb
                                v.tensor_tensor(
                                    ps_s[:, slot * 1024:(slot + 1) * 1024],
                                    ps_s[:, slot * 1024:(slot + 1) * 1024],
                                    mask_b[:, wj * 1024:(wj + 1) * 1024], ALU.add)
                            BV(f)
                            mkv = E.n["vch"]

                        def f(sc, h=h, w=w, slot=slot, mkv=mkv):
                            if mkv is not None:
                                E.w(sc._e, "vch", mkv)
                            else:
                                E.w(sc._e, "sp", spm[(h, w)])
                            key = (h, w - 2) if w >= 2 else (h - 1, NW + w - 2)
                            if key in pvm:
                                E.w(sc._e, "pv", pvm[key])
                                E.w(sc._e, "vch", lam[key])
                            sc.activation(at_pt[:, slot, :],
                                          ps_s[:, slot * 1024:(slot + 1) * 1024],
                                          AF.Exp, scale=float(SCALE),
                                          bias=expb_t[:, h:h + 1])
                        BS(f)
                        exm[(h, w)] = E.n["ach"]

                        def f(v, h=h, w=w, slot=slot):
                            E.w(v._e, "ach", exm[(h, w)])
                            if w == 0:
                                E.w(v._e, "tl", last_m1[0])
                                v.tensor_copy(at_acc, at_pt[:, slot, 0:512])
                            else:
                                v.tensor_tensor(at_acc, at_acc,
                                                at_pt[:, slot, 0:512], ALU.add)
                            v.tensor_tensor(at_acc, at_acc,
                                            at_pt[:, slot, 512:1024], ALU.add)
                        BV(f)
                        lam[(h, w)] = E.n["vch"]

                        @block.tensor
                        def _(te, h=h, w=w, slot=slot, par=par):
                            E.w(te, "ach", exm[(h, w)])
                            if w == 0 and h >= 2:
                                E.w(te, "vch", oem[h - 2])
                            last = None
                            for j in range(2):
                                kc = 2 * w + j
                                last = te.matmul(
                                    ps_o[:, par * 512:(par + 1) * 512],
                                    V_sb[:, kc * 128:(kc + 1) * 128],
                                    at_pt[:, slot, j * 512:(j + 1) * 512],
                                    start=(w == 0 and j == 0),
                                    stop=(w == NW - 1 and j == 1))
                            pvm[(h, w)] = E.inc("pv", last)

                    @block.tensor
                    def _(te, h=h):
                        E.w(te, "vch", lam[(h, NW - 1)])
                        E.w(te, "tl", E.n["tl"])
                        E.w(te, "ach", E.n["ach"])
                        E.inc("tl", te.matmul(ps_m[0:1, :], ones_f[:, :], at_acc,
                                              start=True, stop=True,
                                              tile_position=(0, 0)))
                    m1 = E.n["tl"]
                    last_m1[0] = m1

                    def f(sc, h=h, m1=m1):
                        E.w(sc._e, "tl", m1)
                        if h >= 1:
                            E.w(sc._e, "vch", oem[h - 1])
                        sc.activation(lnl, ps_m[0:1, :], AF.Ln)
                    BS(f)
                    m2 = E.n["ach"]

                    @block.tensor
                    def _(te, m2=m2):
                        E.w(te, "ach", m2)
                        E.inc("tl", te.matmul(ps_m, ones1_f[:, :], lnl,
                                              start=True, stop=True,
                                              tile_position=(0, 0)))
                    m3 = E.n["tl"]

                    def f(sc, m3=m3):
                        E.w(sc._e, "tl", m3)
                        sc.activation(at_rb, ps_m, AF.Exp, scale=-1.0)
                    BS(f)
                    m4 = E.n["ach"]

                    def f(v, h=h, par=par, m4=m4):
                        E.w(v._e, "ach", m4)
                        E.w(v._e, "pv", pvm[(h, NW - 1)])
                        if h >= 2:
                            E.w(v._e, odm[h - 2][0], odm[h - 2][1])
                        v.tensor_tensor(at_ot[:, par, :],
                                        ps_o[:, par * 512:(par + 1) * 512],
                                        at_rb, ALU.mult)
                    BV(f)
                    oem[h] = E.n["vch"]

                    @block.gpsimd
                    def _(g, qb=qb, h=h, par=par):
                        E.w(g, "vch", oem[h])
                        if qb > 0 and h < 2:
                            E.w(g, "woa", E.n["woa"])
                            E.w(g, "wob", E.n["wob"])
                        nm = "oda" if par == 0 else "odb"
                        odm[h] = (nm, E.inc(nm, g.dma_start(
                            out=o_c.ap()[h, :, :], in_=at_ot[:, par, :]), 16))

                wmm = {}
                ev_marks = []
                for nb in range(8):
                    for hc in range(NH):
                        w_nm = "wda" if hc % 2 == 0 else "wdb"
                        o_nm = "woa" if hc % 2 == 0 else "wob"

                        @block.sync
                        def _(s, nb=nb, hc=hc, w_nm=w_nm):
                            if nb == 0 and hc == 0:
                                E.w(s, "oda", E.n["oda"])
                                E.w(s, "odb", E.n["odb"])
                                E.w(s, "pre", m_wo)
                            E.w(s, "wm", max(0, E.n["wm"] - 1))
                            E.inc(w_nm, s.dma_start(
                                out=at_wt[:, hc % 2, :],
                                in_=wo_bf.ap()[hc * 128:(hc + 1) * 128,
                                               nb * 512:(nb + 1) * 512]), 16)

                        @block.scalar
                        def _(sc, nb=nb, hc=hc, o_nm=o_nm):
                            if nb == 0 and hc < 2:
                                E.w(sc, "oda", E.n["oda"])
                                E.w(sc, "odb", E.n["odb"])
                            E.w(sc, "wm", max(0, E.n["wm"] - 1))
                            E.inc(o_nm, sc.dma_start(
                                out=at_oc[:, hc % 2, :], in_=o_c.ap()[hc, :, :]), 16)
                        m_wd, m_wd2 = E.n[w_nm], E.n[o_nm]

                        @block.tensor
                        def _(te, nb=nb, hc=hc, m_wd=m_wd, m_wd2=m_wd2,
                              w_nm=w_nm, o_nm=o_nm):
                            E.w(te, w_nm, m_wd)
                            E.w(te, o_nm, m_wd2)
                            if hc == 0:
                                E.w(te, "ach", E.n["ach"])
                                if ev_marks:
                                    E.w(te, "vch", ev_marks[-1])
                            last = None
                            for qt in range(4):
                                last = te.matmul(
                                    ps_s[:, qt * 512:(qt + 1) * 512],
                                    at_oc[:, hc % 2, qt * 128:(qt + 1) * 128],
                                    at_wt[:, hc % 2, :],
                                    start=(hc == 0), stop=(hc == NH - 1))
                            wmm[(nb, hc)] = E.inc("wm", last)

                    def f(v, qb=qb, nb=nb):
                        E.w(v._e, "wm", wmm[(nb, NH - 1)])
                        for qt in range(4):
                            t = qb * 4 + qt
                            v.scalar_tensor_tensor(
                                res[:, t, nb * 512:(nb + 1) * 512],
                                ps_s[:, qt * 512:(qt + 1) * 512], 1.0,
                                res[:, t, nb * 512:(nb + 1) * 512],
                                ALU.mult, ALU.add)
                    BV(f)
                    ev_marks.append(E.n["vch"])
                prev_wo_mm[0] = E.n["wm"]
                prev_wo_ev[0] = E.n["vch"]

        def emit_images():
            im_buf = A.bf(0, 8192).rearrange("p (sl s) -> p sl s", sl=2)
            jobs = []
            for (nat, img) in ((g_nat, gT), (u_nat, uT)):
                for hb in range(32):
                    for c in range(3):
                        r0 = c * 4096
                        r1 = min(INTER, r0 + 4096)
                        jobs.append((nat, img, hb * 128, r0, r1))
            for ic in range(INTER // 128):
                jobs.append((d_nat, dT, ic * 128, 0, 4096))

            wrm = []
            for i, (nat, img, c0, r0, r1) in enumerate(jobs):
                sl = i % 2

                t_nm = "imt" if sl == 0 else "imu"
                g_nm = "imga" if sl == 0 else "imgb"

                @block.scalar
                def _(sc, nat=nat, c0=c0, r0=r0, r1=r1, sl=sl, i=i, t_nm=t_nm,
                      g_nm=g_nm):
                    if i == 0:
                        E.w(sc, "pre", n_pre_cast)
                    if i >= 2:
                        E.w(sc, g_nm, wrm[i - 2][1])
                    E.inc(t_nm, sc.dma_start(out=im_buf[:, sl, 0:r1 - r0],
                                             in_=nat.ap()[r0:r1, c0:c0 + 128],
                                             transpose=True), 16)
                m_t = E.n[t_nm]

                @block.sync
                def _(s, img=img, c0=c0, r0=r0, r1=r1, sl=sl, m_t=m_t, t_nm=t_nm,
                      g_nm=g_nm):
                    E.w(s, t_nm, m_t)
                    wrm.append((g_nm, E.inc(g_nm, s.dma_start(
                        out=img.ap()[c0:c0 + 128, r0:r1],
                        in_=im_buf[:, sl, 0:r1 - r0]), 16)))
            return (E.n["imga"], E.n["imgb"], E.n["imt"], E.n["imu"])

        def emit_ffn(m_imgs):
            f_x = A.bf(0, 2048).rearrange("p (sl s) -> p sl s", sl=2)
            f_w = A.bf(2048, 8192).rearrange("p (sl m b i) -> p sl m b i",
                                             sl=2, m=2, b=8)
            f_it = A.bf(10240, 4096).rearrange("p (ic s) -> p ic s", ic=4)
            f_sl = A.bf(14336, 1024)
            f_dw = A.bf(15360, 8192).rearrange("p (ic s) -> p ic s", ic=4)

            fe_mark = [E.n["vch"]]
            fs_mark = [E.n["vch"]]
            i0 = 0
            for ib, isz in enumerate(IB_SIZES):
                n_it = isz // 128
                for ig in range(n_it // 2):
                    ibase = i0 + ig * 256
                    wbm, xbm, gm = {}, {}, []
                    for hbB in range(4):
                        @block.sync
                        def _(s, ibase=ibase, hbB=hbB, ib=ib, ig=ig):
                            if ib == 0 and ig == 0 and hbB == 0:
                                E.w(s, "imga", m_imgs[0])
                                E.w(s, "imgb", m_imgs[1])
                                E.w(s, "imt", m_imgs[2])
                                E.w(s, "imu", m_imgs[3])
                                E.w(s, "lx", E.n["lx"])
                            if hbB >= 2:
                                E.w(s, "sp", gm[(hbB - 2) * 8 + 7])
                            elif ib + ig > 0:
                                E.w(s, "sp", E.n["sp"])
                            nm = "fwa" if hbB % 2 == 0 else "fwb"
                            E.inc(nm, s.dma_start(
                                out=f_w[:, hbB % 2, 0, :, :],
                                in_=gT.ap()[hbB * 1024:(hbB + 1) * 1024,
                                            ibase:ibase + 256]
                                    .rearrange("(b p) i -> p b i", p=128)), 16)

                        @block.scalar
                        def _(sc, ibase=ibase, hbB=hbB, ib=ib, ig=ig):
                            if ib == 0 and ig == 0 and hbB == 0:
                                E.w(sc, "imga", m_imgs[0])
                                E.w(sc, "imgb", m_imgs[1])
                                E.w(sc, "imt", m_imgs[2])
                                E.w(sc, "imu", m_imgs[3])
                                E.w(sc, "lx", E.n["lx"])
                            if hbB >= 2:
                                E.w(sc, "sp", gm[(hbB - 2) * 8 + 7])
                            elif ib + ig > 0:
                                E.w(sc, "sp", E.n["sp"])
                            nm = "fua" if hbB % 2 == 0 else "fub"
                            E.inc(nm, sc.dma_start(
                                out=f_w[:, hbB % 2, 1, :, :],
                                in_=uT.ap()[hbB * 1024:(hbB + 1) * 1024,
                                            ibase:ibase + 256]
                                    .rearrange("(b p) i -> p b i", p=128)), 16)
                        par2 = hbB % 2
                        wbm[hbB] = (("fwa" if par2 == 0 else "fwb",
                                     E.n["fwa" if par2 == 0 else "fwb"]),
                                    ("fua" if par2 == 0 else "fub",
                                     E.n["fua" if par2 == 0 else "fub"]))

                        for b in range(8):
                            hb = hbB * 8 + b

                            @block.sync
                            def _(s, hb=hb, ib=ib, ig=ig):
                                if hb >= 2:
                                    E.w(s, "sp", gm[hb - 2])
                                elif ib == 0 and ig == 0:
                                    E.w(s, "lx", E.n["lx"])
                                    E.w(s, "imga", m_imgs[0])
                                    E.w(s, "imgb", m_imgs[1])
                                    E.w(s, "imt", m_imgs[2])
                                    E.w(s, "imu", m_imgs[3])
                                else:
                                    E.w(s, "sp", E.n["sp"])
                                nm = "fxa" if hb % 2 == 0 else "fxb"
                                xbm[hb] = (nm, E.inc(nm, s.dma_start(
                                    out=f_x[:, hb % 2, :], in_=xt2.ap()[hb, :, :]), 16))

                            @block.tensor
                            def _(te, hb=hb, hbB=hbB, b=b, ib=ib, ig=ig):
                                E.w(te, xbm[hb][0], xbm[hb][1])
                                E.w(te, wbm[hbB][0][0], wbm[hbB][0][1])
                                E.w(te, wbm[hbB][1][0], wbm[hbB][1][1])
                                if hb == 0:
                                    if ig == 0:
                                        E.w(te, "vch", fe_mark[0])
                                    else:
                                        E.w(te, "vch", fs_mark[0])
                                last = None
                                for it2 in range(2):
                                    for mat in range(2):
                                        base = (it2 * 2 + mat) * 1024
                                        for half in range(2):
                                            last = te.matmul(
                                                pool[:, base + half * 512:
                                                     base + half * 512 + 512],
                                                f_w[:, hbB % 2, mat, b,
                                                    it2 * 128:(it2 + 1) * 128],
                                                f_x[:, hb % 2,
                                                    half * 512:half * 512 + 512],
                                                start=(hb == 0), stop=(hb == 31))
                                gm.append(E.inc("sp", last))

                    for it2 in range(2):
                        it = ig * 2 + it2

                        def f(sc, it2=it2, gm=gm):
                            E.w(sc._e, "sp", gm[31])
                            if it2 == 1:
                                E.w(sc._e, "vch", fs_mark[0])
                            sc.activation(f_sl, pool[:, it2 * 2048:it2 * 2048 + 1024],
                                          AF.Silu, scale=gsb[:, :])
                        BS(f)
                        m_sl = E.n["ach"]

                        def f(v, it=it, it2=it2, m_sl=m_sl):
                            E.w(v._e, "ach", m_sl)
                            E.w(v._e, "fd", E.n["fd"])
                            v.scalar_tensor_tensor(
                                f_it[:, it, :],
                                pool[:, it2 * 2048 + 1024:it2 * 2048 + 2048],
                                usb[:, :], f_sl, ALU.mult, ALU.mult)
                        BV(f)
                        fs_mark[0] = E.n["vch"]

                for half in range(2):
                    @block.sync
                    def _(s, i0=i0, n_it=n_it, half=half):
                        E.w(s, "fd", E.n["fd"])
                        for ic in range(n_it):
                            E.inc("dw", s.dma_start(
                                out=f_dw[:, ic, :],
                                in_=dT.ap()[i0 + ic * 128:i0 + (ic + 1) * 128,
                                            half * 2048:(half + 1) * 2048]), 16)
                    m_dw = E.n["dw"]

                    fdm = {}
                    dn_ev = {}
                    for st in range(8):
                        @block.tensor
                        def _(te, st=st, half=half, n_it=n_it, m_dw=m_dw):
                            E.w(te, "dw", m_dw)
                            if st == 0 and half == 0:
                                E.w(te, "vch", fs_mark[0])
                            if st >= 2:
                                E.w(te, "vch", dn_ev[st - 2])
                            elif half == 1 and st < 2:
                                E.w(te, "vch", fe_mark[0])
                            sl = (st % 2) * 2048
                            last = None
                            for ic in range(n_it):
                                for nn in range(4):
                                    last = te.matmul(
                                        pool[:, sl + nn * 512:sl + (nn + 1) * 512],
                                        f_it[:, ic, st * 128:(st + 1) * 128],
                                        f_dw[:, ic, nn * 512:(nn + 1) * 512],
                                        start=(ic == 0), stop=(ic == n_it - 1))
                            fdm[st] = E.inc("fd", last)

                        def f(v, st=st, half=half):
                            E.w(v._e, "fd", fdm[st])
                            sl = (st % 2) * 2048
                            v.scalar_tensor_tensor(
                                res[:, st, half * 2048:(half + 1) * 2048],
                                pool[:, sl:sl + 2048], dsb[:, :],
                                res[:, st, half * 2048:(half + 1) * 2048],
                                ALU.mult, ALU.add)
                        BV(f)
                        dn_ev[st] = E.n["vch"]
                    fe_mark[0] = E.n["vch"]
                i0 += isz

        KT_sb = A.bf(0, NKCL * 128)
        V_sb = A.bf(9216, NKCL * 128)
        lp_sb = A.bf(18432, NKCL * 128)
        rp_sb = A.bf(27648, 4096)

        def emit_attn(wq, wkv, wo):
            emit_ln()
            m_kv = emit_kv(wkv, KT_sb, V_sb)
            emit_qproj(wq, m_kv)
            barrier()
            emit_attn_main(KT_sb, V_sb, lp_sb, rp_sb, wo, m_kv)
            barrier()

        emit_attn(W["wq_a"], W["wkv_a"], W["wo_a"])
        m_imgs = emit_images()
        if n_sub >= 2:
            barrier()
            emit_ln()
            emit_ffn(m_imgs)
            barrier()
        if n_sub >= 3:
            emit_attn(W["wq_b"], W["wkv_b"], W["wo_b"])
        if n_sub >= 4:
            emit_ln()
            emit_ffn(m_imgs)
            barrier()

        @block.sync
        def _(s):
            E.w(s, "vch", E.n["vch"])
            E.w(s, "ach", E.n["ach"])
            E.w(s, "imga", E.n["imga"])
            E.w(s, "imgb", E.n["imgb"])
            E.w(s, "pre", E.n["pre"])
            E.inc("st", s.dma_start(out=out.rearrange("(t p) d -> p t d", p=128),
                                    in_=res[:, :, :]), 16)
            s.wait_ge(E.h["st"], E.n["st"])

    return nc


def _in_maps(inputs):
    xs = np.asarray(inputs["L0_attn_a_hidden_states"], np.float32)[0]
    base = {
        "wq_a": np.ascontiguousarray(np.asarray(inputs["L0_attn_a_Wq"], np.float32)),
        "wkv_a": np.ascontiguousarray(np.asarray(inputs["L0_attn_a_Wkv"], np.float32)),
        "wo_a": np.ascontiguousarray(np.asarray(inputs["L0_attn_a_Wo"], np.float32)),
        "wq_b": np.ascontiguousarray(np.asarray(inputs["L0_attn_b_Wq"], np.float32)),
        "wkv_b": np.ascontiguousarray(np.asarray(inputs["L0_attn_b_Wkv"], np.float32)),
        "wo_b": np.ascontiguousarray(np.asarray(inputs["L0_attn_b_Wo"], np.float32)),
        "gate_w": np.ascontiguousarray(np.asarray(inputs["L0_ffn_gate_w"], np.float32)),
        "up_w": np.ascontiguousarray(np.asarray(inputs["L0_ffn_up_w"], np.float32)),
        "down_w": np.ascontiguousarray(np.asarray(inputs["L0_ffn_down_w"], np.float32)),
        "gs_b": np.full((128, 1), np.float32(inputs["L0_ffn_gate_scale"]), np.float32),
        "us_b": np.full((128, 1), np.float32(inputs["L0_ffn_up_scale"]), np.float32),
        "ds_b": np.full((128, 1), np.float32(inputs["L0_ffn_down_scale"]), np.float32),
        "ident": np.eye(128, dtype=np.float32),
    }
    maps = []
    for c in range(NCORES):
        t = build_tables(c)
        m = dict(base)
        m["x"] = np.ascontiguousarray(xs[c * PER:(c + 1) * PER])
        m["t_lpack"] = t["lpack"]
        m["t_rpack"] = t["rpack"]
        m["t_expb"] = t["expb"]
        m["t_mask"] = t["mask"]
        maps.append(m)
    return maps


def kernel(**inputs):
    maps = _in_maps(inputs)
    nc = build_nc(NSUB, DUMP)
    res = run_bass_kernel_spmd(nc, maps, core_ids=list(range(NCORES)))
    outp = np.concatenate([res.results[c]["out"] for c in range(NCORES)], axis=0)
    return outp.reshape(1, SEQ, D)


# version 03 (final)
# speedup vs baseline: 1.3422x; 1.3422x over previous
import math
import numpy as np
import ml_dtypes
import concourse.bass as bass
import concourse.mybir as mybir
from concourse.bass_utils import run_bass_kernel_spmd

F32 = mybir.dt.float32
BF16 = mybir.dt.bfloat16
AF = mybir.ActivationFunctionType
ALU = mybir.AluOpType
AX = mybir.AxisListType

NCORES = 8
D = 4096
SEQ = 8192
PER = SEQ // NCORES
NT = PER // 128
NH = 32
HD = 128
INTER = 11008
SCALE = HD ** -0.5
EPS = 1e-5
NKC = SEQ // HD
NKCL = NKC + 8
NQB = PER // 512
NW = NKCL // 2
IB_SIZES = [512] * 21 + [256]
BF = ml_dtypes.bfloat16

NSUB = 4
DUMP = False


def _slopes():
    base = 2.0 ** (-(2.0 ** -(math.log2(NH) - 3)))
    return [base ** (h + 1) for h in range(NH)]


def _split3(x):
    hi = x.astype(BF).astype(np.float64)
    mid = (x - hi).astype(BF).astype(np.float64)
    lo = (x - hi - mid).astype(BF).astype(np.float64)
    return hi, mid, lo


def _split2(x):
    hi = x.astype(BF).astype(np.float64)
    lo = (x - hi).astype(BF).astype(np.float64)
    return hi, lo


def build_tables(core):
    slopes = _slopes()
    q_glob0 = core * PER
    k0_of_kc = [q_glob0 + kc * HD for kc in range(8)] + \
               [(kc - 8) * HD for kc in range(8, NKCL)]

    lpack = np.zeros((NQB, 128, NKCL * 128), dtype=BF)
    for qb in range(NQB):
        q0 = q_glob0 + qb * 512
        for kc in range(NKCL):
            k0 = k0_of_kc[kc]
            dup = kc >= 8 and (kc - 8) // 8 == core
            if dup or k0 >= q0 + 512:
                rows = np.array([-1e10, 0, 0, -1e10, 0, 0, 0, 0], np.float64)
            else:
                sp = _split3(np.array([k0 / SCALE], np.float64))
                rows = np.array([sp[0][0], sp[1][0], sp[2][0],
                                 sp[0][0], sp[1][0], 1, 1, 1], np.float64)
            tile = np.repeat(rows[:, None], 128, axis=1).astype(BF)
            for b in range(4):
                lpack[qb, 32 * b:32 * b + 8, kc * 128:(kc + 1) * 128] = tile

    rpack = np.zeros((NQB, 128, 8 * 512), dtype=BF)
    for qb in range(NQB):
        q0 = q_glob0 + qb * 512
        f = np.arange(512, dtype=np.float64)
        for h in range(NH):
            sp = _split2(np.array([slopes[h]], np.float64))
            s_hi, s_lo = float(sp[0][0]), float(sp[1][0])
            v_hi, v_mid, v_lo = _split3(-slopes[h] * (q0 + f) / SCALE)
            rows = np.stack([np.full(512, s_hi), np.full(512, s_hi),
                             np.full(512, s_hi), np.full(512, s_lo),
                             np.full(512, s_lo), v_hi, v_mid, v_lo])
            pb, fo = 32 * (h % 4), 512 * (h // 4)
            rpack[qb, pb:pb + 8, fo:fo + 512] = rows.astype(BF)

    expb = np.empty((128, NH), dtype=np.float32)
    for h in range(NH):
        expb[:, h] = np.float32(slopes[h]) * np.arange(128, dtype=np.float32)

    mask = np.zeros((128, 2048), dtype=np.float32)
    p = np.arange(128)[:, None]
    f = np.arange(512)[None, :]
    for j in range(4):
        mask[:, j * 512:(j + 1) * 512] = np.where(f < 128 * j + p, -1e9, 0.0)
    return dict(lpack=lpack, rpack=rpack, expb=expb, mask=mask.astype(BF))


class Arena:
    def __init__(self, tensor, total):
        self.t = tensor
        self.total = total

    def bf(self, off, elems):
        assert off + elems <= self.total
        return self.t[:, off:off + elems]

    def f32(self, off, elems):
        assert off % 2 == 0 and off + 2 * elems <= self.total
        return self.t[:, off:off + 2 * elems].bitcast(F32)


class P:
    def __init__(self):
        self.h = {}
        self.n = {}

    def sem(self, name, handle):
        self.h[name] = handle
        self.n[name] = 0

    def inc(self, name, instr, k=1):
        instr.then_inc(self.h[name], k)
        self.n[name] += k
        return self.n[name]

    def w(self, eng, name, val):
        if val > 0:
            eng.wait_ge(self.h[name], val)


class Chain:
    OPS = {"tensor_tensor", "tensor_copy", "tensor_add", "tensor_sub",
           "tensor_mul", "tensor_max", "tensor_scalar", "tensor_scalar_mul",
           "tensor_scalar_add", "tensor_single_scalar", "scalar_tensor_tensor",
           "reduce_sum", "reduce_max", "tensor_reduce", "memset", "memzero",
           "reciprocal", "activation", "copy", "add", "mul", "square",
           "transpose", "select"}

    def __init__(self, E, eng, sem):
        self._E = E
        self._e = eng
        self._s = sem

    def __getattr__(self, name):
        attr = getattr(self._e, name)
        if name not in Chain.OPS:
            return attr

        def wrapped(*a, **kw):
            E = self._E
            E.w(self._e, self._s, E.n[self._s])
            instr = attr(*a, **kw)
            return E.inc(self._s, instr)
        return wrapped


SEMS = ["ld", "lg", "ini", "pre", "img", "imt", "imu", "imga", "imgb",
        "bar", "lt", "lx", "kv", "kva", "kvb", "kg", "cc", "qpa", "qpb",
        "qga", "qgb", "qe", "qa", "qb", "rs", "sp", "pv", "tl", "od",
        "wda", "wdb", "woa", "wob", "wm", "fxa", "fxb", "fwa", "fwb",
        "fua", "fub", "dw", "fd", "st", "vch", "ach", "oda", "odb"]


def build_nc(n_sub=NSUB, dump=DUMP):
    nc = bass.Bass("TRN2", target_bir_lowering=False, debug=False)
    E = P()

    x_in = nc.dram_tensor("x", [PER, D], F32, kind="ExternalInput").ap()
    out = nc.dram_tensor("out", [PER, D], F32, kind="ExternalOutput").ap()
    W = {}
    for nm, shp in (("wq_a", [D, D]), ("wkv_a", [D, 256]), ("wo_a", [D, D]),
                    ("wq_b", [D, D]), ("wkv_b", [D, 256]), ("wo_b", [D, D]),
                    ("gate_w", [INTER, D]), ("up_w", [INTER, D]),
                    ("down_w", [D, INTER])):
        W[nm] = nc.dram_tensor(nm, shp, F32, kind="ExternalInput").ap()
    gs_b = nc.dram_tensor("gs_b", [128, 1], F32, kind="ExternalInput").ap()
    us_b = nc.dram_tensor("us_b", [128, 1], F32, kind="ExternalInput").ap()
    ds_b = nc.dram_tensor("ds_b", [128, 1], F32, kind="ExternalInput").ap()
    t_lpack = nc.dram_tensor("t_lpack", [NQB, 128, NKCL * 128], BF16, kind="ExternalInput").ap()
    t_rpack = nc.dram_tensor("t_rpack", [NQB, 128, 8 * 512], BF16, kind="ExternalInput").ap()
    t_expb = nc.dram_tensor("t_expb", [128, NH], F32, kind="ExternalInput").ap()
    t_mask = nc.dram_tensor("t_mask", [128, 2048], BF16, kind="ExternalInput").ap()
    ident_in = nc.dram_tensor("ident", [128, 128], F32, kind="ExternalInput").ap()

    gT = nc.dram_tensor("gT", [D, INTER], BF16)
    uT = nc.dram_tensor("uT", [D, INTER], BF16)
    dT = nc.dram_tensor("dT", [INTER, D], BF16)
    g_nat = nc.dram_tensor("g_nat", [INTER, D], BF16)
    u_nat = nc.dram_tensor("u_nat", [INTER, D], BF16)
    d_nat = nc.dram_tensor("d_nat", [D, INTER], BF16)
    wo_bf = nc.dram_tensor("wo_bf", [D, D], BF16)
    xt2 = nc.dram_tensor("xt2", [32, 128, PER], BF16)
    qt_c = nc.dram_tensor("qt_c", [NH, 128, PER], BF16)
    o_c = nc.dram_tensor("o_c", [NH, 128, 512], BF16)
    kt_own = nc.dram_tensor("kt_own", [128, PER], BF16)
    v_own = nc.dram_tensor("v_own", [PER, HD], BF16)
    kt_g = nc.dram_tensor("kt_g", [NCORES * 128, PER], BF16)
    v_g = nc.dram_tensor("v_g", [SEQ, HD], BF16)

    ARENA = 37888

    import contextlib
    with contextlib.ExitStack() as STK:
        res = STK.enter_context(nc.sbuf_tensor("res", [128, NT, D], F32))
        ident_f = STK.enter_context(nc.sbuf_tensor("ident_f", [128, 128], F32))
        ones_f = STK.enter_context(nc.sbuf_tensor("ones_f", [128, 1], F32))
        ones1_f = STK.enter_context(nc.sbuf_tensor("ones1_f", [1, 128], F32))
        gsb = STK.enter_context(nc.sbuf_tensor("gsb", [128, 1], F32))
        usb = STK.enter_context(nc.sbuf_tensor("usb", [128, 1], F32))
        dsb = STK.enter_context(nc.sbuf_tensor("dsb", [128, 1], F32))
        expb_t = STK.enter_context(nc.sbuf_tensor("expb_t", [128, NH], F32))
        mask_b = STK.enter_context(nc.sbuf_tensor("mask_b", [128, 2048], BF16))
        epsb = STK.enter_context(nc.sbuf_tensor("epsb", [128, 1], F32))
        ln_st = STK.enter_context(nc.sbuf_tensor("ln_st", [128, 8], F32))
        ar_t = STK.enter_context(nc.sbuf_tensor("arena", [128, ARENA], BF16))
        pool = STK.enter_context(nc.psum_tensor("pool", [128, 4096], F32))
        for nm in SEMS:
            E.sem(nm, STK.enter_context(nc.semaphore("s_" + nm)))
        block = STK.enter_context(nc.Block())
        A = Arena(ar_t, ARENA)

        def BV(f):
            block.vector(lambda v: f(Chain(E, v, "vch")))

        def BS(f):
            block.scalar(lambda sc: f(Chain(E, sc, "ach")))

        @block.sync
        def _(s):
            E.inc("ld", s.dma_start(out=res[:, :, :],
                                    in_=x_in.rearrange("(t p) d -> p t d", p=128)), 16)
            E.inc("ld", s.dma_start(out=mask_b[:, :], in_=t_mask), 16)
            E.inc("ld", s.dma_start(out=expb_t[:, :], in_=t_expb), 16)

        @block.gpsimd
        def _(g):
            E.inc("lg", g.dma_start(out=ident_f[:, :], in_=ident_in), 16)
            E.inc("lg", g.dma_start(out=gsb[:, :], in_=gs_b), 16)
            E.inc("lg", g.dma_start(out=usb[:, :], in_=us_b), 16)
            E.inc("lg", g.dma_start(out=dsb[:, :], in_=ds_b), 16)

        def _init(v):
            v.memset(ones_f[:, :], 1.0)
            v.memset(epsb[:, :], float(EPS))
            v.memset(ones1_f[:, :], 1.0)
        BV(_init)
        n_ld, n_lg, n_ini = E.n["ld"], E.n["lg"], E.n["vch"]

        @block.gpsimd
        def _(g):
            for (src, dst, rows) in ((W["gate_w"], g_nat, INTER),
                                     (W["up_w"], u_nat, INTER),
                                     (W["down_w"], d_nat, D)):
                step = rows // 8
                for c in range(8):
                    E.inc("pre", g.dma_start(out=dst.ap()[c * step:(c + 1) * step, :],
                                             in_=src[c * step:(c + 1) * step, :]), 16)
        n_pre_cast = E.n["pre"]

        BAR_SEMS = ["lt", "sp", "pv", "tl", "wm", "fd", "vch", "ach",
                    "ld", "lg", "lx", "kv", "kva", "kvb", "kg", "qpa", "qpb",
                    "qga", "qgb", "qe", "qa", "qb", "rs", "oda", "odb", "wda", "wdb",
                    "woa", "wob", "fxa", "fxb", "fwa", "fwb", "fua", "fub",
                    "dw", "pre", "imt", "imu", "imga", "imgb", "cc"]

        def barrier():
            vals = [(nm, E.n[nm]) for nm in BAR_SEMS if E.n[nm] > 0]
            for eng in (block.tensor, block.vector, block.scalar, block.sync,
                        block.gpsimd):
                @eng
                def _(e):
                    for nm, val in vals:
                        e.wait_ge(E.h[nm], val)

        def emit_ln():
            ln_sq = A.bf(0, 4096)
            ln_xb = A.f32(4096, 4096)
            ln_ts = A.bf(12288, 4096)
            last_ident = [0]
            last_store = [0]
            last_tp = [0]
            for t in range(NT):
                def f(v, t=t):
                    if t == 0:
                        E.w(v._e, "ld", n_ld)
                        E.w(v._e, "lg", n_lg)
                    else:
                        E.w(v._e, "ach", last_ident[0])
                    v.reduce_sum(ln_st[:, 0:1], res[:, t, :], axis=AX.X)
                BV(f)
                m_v1 = E.n["vch"]

                def f(sc, t=t, m_v1=m_v1):
                    E.w(sc._e, "vch", m_v1)
                    sc.activation(ln_sq, res[:, t, :], AF.Square,
                                  accum_out=ln_st[:, 1:2])
                BS(f)
                m_a1 = E.n["ach"]

                def f(v, m_a1=m_a1):
                    E.w(v._e, "ach", m_a1)
                    v.tensor_scalar_mul(ln_st[:, 2:3], ln_st[:, 0:1], 1.0 / D)
                    v.tensor_scalar_mul(ln_st[:, 3:4], ln_st[:, 1:2], 1.0 / D)
                    v.tensor_mul(ln_st[:, 4:5], ln_st[:, 2:3], ln_st[:, 2:3])
                    v.tensor_sub(ln_st[:, 5:6], ln_st[:, 3:4], ln_st[:, 4:5])
                BV(f)
                m_v2 = E.n["vch"]

                def f(sc, m_v2=m_v2):
                    E.w(sc._e, "vch", m_v2)
                    sc.activation(ln_st[:, 6:7], ln_st[:, 5:6], AF.Ln, bias=epsb[:, :])
                    sc.activation(ln_st[:, 6:7], ln_st[:, 6:7], AF.Exp, scale=-0.5)
                BS(f)
                m_a2 = E.n["ach"]

                def f(v, m_a2=m_a2):
                    E.w(v._e, "ach", m_a2)
                    v.tensor_mul(ln_st[:, 7:8], ln_st[:, 2:3], ln_st[:, 6:7])
                    v.tensor_scalar_mul(ln_st[:, 7:8], ln_st[:, 7:8], -1.0)
                BV(f)
                m_v3 = E.n["vch"]

                def f(sc, t=t, m_v3=m_v3):
                    E.w(sc._e, "vch", m_v3)
                    E.w(sc._e, "lt", last_tp[0])
                    sc.activation(ln_xb, res[:, t, :], AF.Identity,
                                  scale=ln_st[:, 6:7], bias=ln_st[:, 7:8])
                BS(f)
                last_ident[0] = E.n["ach"]
                m_a3 = E.n["ach"]

                tms = []
                cms = []
                for gb in range(4):
                    @block.tensor
                    def _(te, gb=gb, m_a3=m_a3):
                        E.w(te, "ach", m_a3)
                        if gb >= 2:
                            E.w(te, "vch", cms[gb - 2])
                        sl = 2048 + (gb % 2) * 1024
                        last = None
                        for b in range(8):
                            hb = gb * 8 + b
                            last = te.transpose(pool[:, sl + b * 128:sl + (b + 1) * 128],
                                                ln_xb[:, hb * 128:(hb + 1) * 128],
                                                ident_f[:, :])
                        tms.append(E.inc("lt", last))

                    def f(v, gb=gb):
                        E.w(v._e, "lt", tms[gb])
                        if gb == 0:
                            E.w(v._e, "lx", last_store[0])
                        sl = 2048 + (gb % 2) * 1024
                        v.tensor_copy(ln_ts[:, gb * 1024:(gb + 1) * 1024],
                                      pool[:, sl:sl + 1024])
                        cms.append(E.n["vch"])
                    BV(f)
                last_tp[0] = tms[3]

                @block.sync
                def _(s, t=t):
                    E.w(s, "vch", cms[3])
                    E.inc("lx", s.dma_start(
                        out=xt2.ap()[:, :, t * 128:(t + 1) * 128]
                            .rearrange("hb p j -> p hb j"),
                        in_=ln_ts.rearrange("p (hb j) -> p hb j", hb=32)), 16)
                last_store[0] = E.n["lx"]

        def emit_kv(wkv, KT_sb, V_sb):
            kv_w = A.bf(18432, 8192).rearrange("p (hb e) -> p hb e", hb=32)
            kv_x0 = A.bf(27648, 2048).rearrange("p (sl s) -> p sl s", sl=2)
            kv_o = A.bf(29696, 2048)
            ps_k = pool[:, 0:1024]
            ps_v = pool[:, 1024:2048]

            @block.gpsimd
            def _(g):
                E.w(g, "lx", E.n["lx"])
                E.inc("kg", g.dma_start(
                    out=kv_w, in_=wkv.rearrange("(hb p) e -> p hb e", p=128)), 16)
            m_w = E.n["kg"]

            xm, mmk = [], []
            for hb in range(32):
                @block.sync
                def _(s, hb=hb):
                    if hb >= 2:
                        E.w(s, "sp", mmk[hb - 2])
                    else:
                        E.w(s, "lx", E.n["lx"])
                        E.w(s, "sp", E.n["sp"])
                    nm = "kva" if hb % 2 == 0 else "kvb"
                    xm.append((nm, E.inc(nm, s.dma_start(
                        out=kv_x0[:, hb % 2, :], in_=xt2.ap()[hb, :, :]), 16)))

                @block.tensor
                def _(te, hb=hb):
                    E.w(te, xm[hb][0], xm[hb][1])
                    if hb == 0:
                        E.w(te, "kg", m_w)
                        E.w(te, "vch", E.n["vch"])
                    sl = hb % 2
                    st0, sp0 = (hb == 0), (hb == 31)
                    for half in range(2):
                        te.matmul(ps_k[:, half * 512:(half + 1) * 512],
                                  kv_w[:, hb, 0:128],
                                  kv_x0[:, sl, half * 512:half * 512 + 512],
                                  start=st0, stop=sp0)
                    last = None
                    for half in range(2):
                        last = te.matmul(ps_v[:, half * 512:(half + 1) * 512],
                                         kv_w[:, hb, 128:256],
                                         kv_x0[:, sl, half * 512:half * 512 + 512],
                                         start=st0, stop=sp0)
                    mmk.append(E.inc("sp", last))

            vt_f = A.f32(27648, 1024)

            def f(v):
                E.w(v._e, "sp", mmk[31])
                v.tensor_copy(kv_o[:, 0:1024], ps_k)
                v.tensor_copy(vt_f, ps_v)
            BV(f)
            m_vt0 = E.n["vch"]

            @block.tensor
            def _(te):
                E.w(te, "vch", m_vt0)
                last = None
                for st in range(8):
                    last = te.transpose(pool[:, 2048 + st * 128:2048 + (st + 1) * 128],
                                        vt_f[:, st * 128:(st + 1) * 128],
                                        ident_f[:, :])
                E.inc("lt", last)
            m_vtp = E.n["lt"]

            def f(v):
                E.w(v._e, "lt", m_vtp)
                v.tensor_copy(kv_o[:, 1024:2048], pool[:, 2048:3072])
            BV(f)
            m_ev = E.n["vch"]

            @block.sync
            def _(s):
                E.w(s, "vch", m_ev)
                E.inc("kv", s.dma_start(out=kt_own.ap(), in_=kv_o[:, 0:1024]), 16)
                E.inc("kv", s.dma_start(
                    out=v_own.ap().rearrange("(st p) d -> p st d", p=128),
                    in_=kv_o[:, 1024:2048].rearrange("p (st d) -> p st d", st=8)), 16)
            m_own = E.n["kv"]

            @block.gpsimd
            def _(g):
                E.w(g, "kv", m_own)
                E.inc("cc", g.collective_compute(
                    "AllGather", ALU.bypass, replica_groups=[list(range(NCORES))],
                    ins=[kt_own.ap().opt()], outs=[kt_g.ap().opt()]))
                E.inc("cc", g.collective_compute(
                    "AllGather", ALU.bypass, replica_groups=[list(range(NCORES))],
                    ins=[v_own.ap().opt()], outs=[v_g.ap().opt()]))
            m_cc = E.n["cc"]

            @block.sync
            def _(s):
                E.w(s, "kv", m_own)
                E.inc("kv", s.dma_start(out=KT_sb[:, 0:1024], in_=kt_own.ap()), 16)
                E.inc("kv", s.dma_start(
                    out=V_sb[:, 0:1024].rearrange("p (kb j) -> p kb j", kb=8),
                    in_=v_own.ap().rearrange("(kb p) j -> p kb j", p=128)), 16)
                E.w(s, "cc", m_cc)
                E.inc("kv", s.dma_start(
                    out=KT_sb[:, 1024:9216].rearrange("p (r s) -> p r s", r=8),
                    in_=kt_g.ap().rearrange("(r p) s -> p r s", p=128)), 16)
                E.inc("kv", s.dma_start(
                    out=V_sb[:, 1024:9216].rearrange("p (kb j) -> p kb j", kb=64),
                    in_=v_g.ap().rearrange("(kb p) j -> p kb j", p=128)), 16)
            return E.n["kv"]

        def emit_qproj(wq, m_kv):
            qp_x = A.bf(31744, 1024).rearrange("p (sl s) -> p sl s", sl=2)
            qp_w = A.bf(32768, 2048).rearrange("p (sl s) -> p sl s", sl=2)
            qp_o = A.bf(18432, 4096).rearrange("p (hh s) -> p hh s", hh=8)
            last_evac = [0]
            last_store = [0]
            for qh in range(2):
                for hg in range(4):
                    xm, wm, mm = [], [], []
                    for hb in range(32):
                        @block.sync
                        def _(s, qh=qh, hb=hb):
                            if hb >= 2:
                                E.w(s, "sp", mm[hb - 2])
                            else:
                                E.w(s, "kv", m_kv)
                                E.w(s, "sp", E.n["sp"])
                            nm = "qpa" if hb % 2 == 0 else "qpb"
                            xm.append((nm, E.inc(nm, s.dma_start(
                                out=qp_x[:, hb % 2, :],
                                in_=xt2.ap()[hb, :, qh * 512:(qh + 1) * 512]), 16)))

                        @block.gpsimd
                        def _(g, hg=hg, hb=hb):
                            if hb >= 2:
                                E.w(g, "sp", mm[hb - 2])
                            else:
                                E.w(g, "kv", m_kv)
                                E.w(g, "sp", E.n["sp"])
                            nm = "qga" if hb % 2 == 0 else "qgb"
                            wm.append((nm, E.inc(nm, g.dma_start(
                                out=qp_w[:, hb % 2, :],
                                in_=wq[hb * 128:(hb + 1) * 128,
                                       hg * 1024:(hg + 1) * 1024]), 16)))

                        @block.tensor
                        def _(te, hb=hb):
                            E.w(te, xm[hb][0], xm[hb][1])
                            E.w(te, wm[hb][0], wm[hb][1])
                            if hb == 0:
                                E.w(te, "vch", last_evac[0])
                            last = None
                            for hh in range(8):
                                last = te.matmul(pool[:, hh * 512:(hh + 1) * 512],
                                                 qp_w[:, hb % 2, hh * 128:(hh + 1) * 128],
                                                 qp_x[:, hb % 2, :],
                                                 start=(hb == 0), stop=(hb == 31))
                            mm.append(E.inc("sp", last))

                    def f(v, mm=mm):
                        E.w(v._e, "sp", mm[31])
                        E.w(v._e, "qe", last_store[0])
                        for hh in range(8):
                            v.tensor_copy(qp_o[:, hh, :],
                                          pool[:, hh * 512:(hh + 1) * 512])
                    BV(f)
                    last_evac[0] = E.n["vch"]
                    m_e = E.n["vch"]

                    @block.sync
                    def _(s, qh=qh, hg=hg, m_e=m_e):
                        E.w(s, "vch", m_e)
                        E.inc("qe", s.dma_start(
                            out=qt_c.ap()[hg * 8:(hg + 1) * 8, :,
                                          qh * 512:(qh + 1) * 512]
                                .rearrange("h p j -> p h j"),
                            in_=qp_o[:, :, :]), 16)
                    last_store[0] = E.n["qe"]

        def emit_attn_main(KT_sb, V_sb, lp_sb, rp_sb, wo_fp32, m_kv):
            at_qt = A.bf(31744, 1024).rearrange("p (sl s) -> p sl s", sl=2)
            at_pt = A.bf(32768, 2048).rearrange("p (sl s) -> p sl s", sl=2)
            at_acc = A.f32(34816, 512)
            at_rb = A.f32(35840, 512)
            lnl = at_rb[0:1, :]
            at_ot = A.bf(36864, 1024).rearrange("p (sl s) -> p sl s", sl=2)
            at_wt = at_qt
            at_oc = at_ot
            ps_s = pool[:, 0:2048]
            ps_o = pool[:, 2048:3072]
            ps_m = pool[:, 3072:3584]

            @block.gpsimd
            def _(g):
                for c in range(8):
                    E.inc("pre", g.dma_start(out=wo_bf.ap()[c * 512:(c + 1) * 512, :],
                                             in_=wo_fp32[c * 512:(c + 1) * 512, :]), 16)
            m_wo = E.n["pre"]

            prev_wo_mm = [0]
            prev_wo_ev = [0]
            for qb in range(NQB):
                @block.sync
                def _(s, qb=qb):
                    if qb == 0:
                        E.w(s, "kv", m_kv)
                    else:
                        E.w(s, "sp", E.n["sp"])
                    E.inc("rs", s.dma_start(out=lp_sb, in_=t_lpack[qb]), 16)
                    E.inc("rs", s.dma_start(out=rp_sb, in_=t_rpack[qb]), 16)
                m_tab = E.n["rs"]

                exm, pvm, lam, oem, odm, spm = {}, {}, {}, {}, {}, {}
                last_m1 = [E.n["tl"]]

                for h in range(NH):
                    par = h % 2

                    q_nm = "qa" if par == 0 else "qb"

                    @block.sync
                    def _(s, qb=qb, h=h, par=par, q_nm=q_nm):
                        if h >= 2:
                            E.w(s, "sp", spm[(h - 2, NW - 1)])
                        elif qb > 0:
                            E.w(s, "sp", E.n["sp"])
                        if qb > 0:
                            E.w(s, "wm", prev_wo_mm[0])
                        E.inc(q_nm, s.dma_start(
                            out=at_qt[:, par, :],
                            in_=qt_c.ap()[h, :, qb * 512:(qb + 1) * 512]), 16)
                    m_q = E.n[q_nm]

                    ach_base = E.n["ach"]
                    for w in range(NW):
                        exm[(h, w)] = ach_base + w + 1
                    mkvs = {}

                    def pv_emit(w, h=h, par=par):
                        slot = w % 2

                        @block.tensor
                        def _(te, h=h, w=w, slot=slot, par=par):
                            E.w(te, "ach", exm[(h, w)])
                            if w == 0 and h >= 2:
                                E.w(te, "vch", oem[h - 2])
                            last = None
                            for j in range(2):
                                kc = 2 * w + j
                                last = te.matmul(
                                    ps_o[:, par * 512:(par + 1) * 512],
                                    V_sb[:, kc * 128:(kc + 1) * 128],
                                    at_pt[:, slot, j * 512:(j + 1) * 512],
                                    start=(w == 0 and j == 0),
                                    stop=(w == NW - 1 and j == 1))
                            pvm[(h, w)] = E.inc("pv", last)

                    for w in range(NW):
                        slot = w % 2
                        diag = w in (2 * qb, 2 * qb + 1)

                        @block.tensor
                        def _(te, qb=qb, h=h, w=w, slot=slot, par=par, m_q=m_q,
                              q_nm=q_nm):
                            if w == 0:
                                E.w(te, q_nm, m_q)
                                if h == 0:
                                    E.w(te, "rs", m_tab)
                                    E.w(te, "qe", E.n["qe"])
                                    if qb > 0:
                                        E.w(te, "vch", prev_wo_ev[0])
                            key = (h, w - 2) if w >= 2 else (h - 1, NW + w - 2)
                            if key in exm:
                                E.w(te, "ach", exm[key])
                            hb4 = 32 * (h % 4)
                            last = None
                            for j in range(2):
                                kc = 2 * w + j
                                dst = ps_s[:, slot * 1024 + j * 512:
                                           slot * 1024 + (j + 1) * 512]
                                te.matmul(dst, KT_sb[:, kc * 128:(kc + 1) * 128],
                                          at_qt[:, par, :], start=True, stop=False)
                                last = te.matmul(
                                    dst, lp_sb[hb4:hb4 + 8, kc * 128:(kc + 1) * 128],
                                    rp_sb[hb4:hb4 + 8,
                                          512 * (h // 4):512 * (h // 4) + 512],
                                    start=False, stop=True, tile_position=(hb4, 0))
                            spm[(h, w)] = E.inc("sp", last)

                        if diag:
                            def f(v, qb=qb, h=h, w=w, slot=slot):
                                E.w(v._e, "sp", spm[(h, w)])
                                wj = w - 2 * qb
                                v.tensor_tensor(
                                    ps_s[:, slot * 1024:(slot + 1) * 1024],
                                    ps_s[:, slot * 1024:(slot + 1) * 1024],
                                    mask_b[:, wj * 1024:(wj + 1) * 1024], ALU.add)
                            BV(f)
                            mkvs[w] = E.n["vch"]

                        def f(v, h=h, w=w, slot=slot):
                            E.w(v._e, "ach", exm[(h, w)])
                            if w == 0:
                                E.w(v._e, "tl", last_m1[0])
                                v.tensor_copy(at_acc, at_pt[:, slot, 0:512])
                            else:
                                v.tensor_tensor(at_acc, at_acc,
                                                at_pt[:, slot, 0:512], ALU.add)
                            v.tensor_tensor(at_acc, at_acc,
                                            at_pt[:, slot, 512:1024], ALU.add)
                        BV(f)
                        lam[(h, w)] = E.n["vch"]

                        if w >= 1:
                            pv_emit(w - 1)
                    pv_emit(NW - 1)

                    @block.tensor
                    def _(te, h=h, ach_base=ach_base):
                        E.w(te, "vch", lam[(h, NW - 1)])
                        E.w(te, "tl", E.n["tl"])
                        E.w(te, "ach", ach_base + NW)
                        E.inc("tl", te.matmul(ps_m[0:1, :], ones_f[:, :], at_acc,
                                              start=True, stop=True,
                                              tile_position=(0, 0)))
                    m1 = E.n["tl"]
                    last_m1[0] = m1

                    @block.tensor
                    def _(te, ach_base=ach_base):
                        E.w(te, "ach", ach_base + NW + 1)
                        E.inc("tl", te.matmul(ps_m, ones1_f[:, :], lnl,
                                              start=True, stop=True,
                                              tile_position=(0, 0)))
                    m3 = E.n["tl"]

                    @block.scalar
                    def _(sc, qb=qb, h=h, m1=m1, m3=m3, mkvs=mkvs):
                        for w in range(NW):
                            slot = w % 2
                            if w in mkvs:
                                E.w(sc, "vch", mkvs[w])
                            else:
                                E.w(sc, "sp", spm[(h, w)])
                            key = (h, w - 2) if w >= 2 else (h - 1, NW + w - 2)
                            if key in pvm:
                                E.w(sc, "pv", pvm[key])
                                E.w(sc, "vch", lam[key])
                            E.inc("ach", sc.activation(
                                at_pt[:, slot, :],
                                ps_s[:, slot * 1024:(slot + 1) * 1024],
                                AF.Exp, scale=float(SCALE),
                                bias=expb_t[:, h:h + 1]))
                        E.w(sc, "tl", m1)
                        if h >= 1:
                            E.w(sc, "vch", oem[h - 1])
                        E.inc("ach", sc.activation(lnl, ps_m[0:1, :], AF.Ln))
                        E.w(sc, "tl", m3)
                        E.inc("ach", sc.activation(at_rb, ps_m, AF.Exp,
                                                   scale=-1.0))
                    m4 = E.n["ach"]

                    def f(v, h=h, par=par, m4=m4):
                        E.w(v._e, "ach", m4)
                        E.w(v._e, "pv", pvm[(h, NW - 1)])
                        if h >= 2:
                            E.w(v._e, odm[h - 2][0], odm[h - 2][1])
                        v.tensor_tensor(at_ot[:, par, :],
                                        ps_o[:, par * 512:(par + 1) * 512],
                                        at_rb, ALU.mult)
                    BV(f)
                    oem[h] = E.n["vch"]

                    @block.gpsimd
                    def _(g, qb=qb, h=h, par=par):
                        E.w(g, "vch", oem[h])
                        if qb > 0 and h < 2:
                            E.w(g, "woa", E.n["woa"])
                            E.w(g, "wob", E.n["wob"])
                        nm = "oda" if par == 0 else "odb"
                        odm[h] = (nm, E.inc(nm, g.dma_start(
                            out=o_c.ap()[h, :, :], in_=at_ot[:, par, :]), 16))

                wmm = {}
                ev_marks = []
                for nb in range(8):
                    for hc in range(NH):
                        w_nm = "wda" if hc % 2 == 0 else "wdb"
                        o_nm = "woa" if hc % 2 == 0 else "wob"

                        @block.sync
                        def _(s, nb=nb, hc=hc, w_nm=w_nm):
                            if nb == 0 and hc == 0:
                                E.w(s, "oda", E.n["oda"])
                                E.w(s, "odb", E.n["odb"])
                                E.w(s, "pre", m_wo)
                            E.w(s, "wm", max(0, E.n["wm"] - 1))
                            E.inc(w_nm, s.dma_start(
                                out=at_wt[:, hc % 2, :],
                                in_=wo_bf.ap()[hc * 128:(hc + 1) * 128,
                                               nb * 512:(nb + 1) * 512]), 16)

                        @block.scalar
                        def _(sc, nb=nb, hc=hc, o_nm=o_nm):
                            if nb == 0 and hc < 2:
                                E.w(sc, "oda", E.n["oda"])
                                E.w(sc, "odb", E.n["odb"])
                            E.w(sc, "wm", max(0, E.n["wm"] - 1))
                            E.inc(o_nm, sc.dma_start(
                                out=at_oc[:, hc % 2, :], in_=o_c.ap()[hc, :, :]), 16)
                        m_wd, m_wd2 = E.n[w_nm], E.n[o_nm]

                        @block.tensor
                        def _(te, nb=nb, hc=hc, m_wd=m_wd, m_wd2=m_wd2,
                              w_nm=w_nm, o_nm=o_nm):
                            E.w(te, w_nm, m_wd)
                            E.w(te, o_nm, m_wd2)
                            if hc == 0:
                                E.w(te, "ach", E.n["ach"])
                                if ev_marks:
                                    E.w(te, "vch", ev_marks[-1])
                            last = None
                            for qt in range(4):
                                last = te.matmul(
                                    ps_s[:, qt * 512:(qt + 1) * 512],
                                    at_oc[:, hc % 2, qt * 128:(qt + 1) * 128],
                                    at_wt[:, hc % 2, :],
                                    start=(hc == 0), stop=(hc == NH - 1))
                            wmm[(nb, hc)] = E.inc("wm", last)

                    def f(v, qb=qb, nb=nb):
                        E.w(v._e, "wm", wmm[(nb, NH - 1)])
                        for qt in range(4):
                            t = qb * 4 + qt
                            v.scalar_tensor_tensor(
                                res[:, t, nb * 512:(nb + 1) * 512],
                                ps_s[:, qt * 512:(qt + 1) * 512], 1.0,
                                res[:, t, nb * 512:(nb + 1) * 512],
                                ALU.mult, ALU.add)
                    BV(f)
                    ev_marks.append(E.n["vch"])
                prev_wo_mm[0] = E.n["wm"]
                prev_wo_ev[0] = E.n["vch"]

        def emit_images():
            im_buf = A.bf(0, 8192).rearrange("p (sl s) -> p sl s", sl=2)
            jobs = []
            for (nat, img) in ((g_nat, gT), (u_nat, uT)):
                for hb in range(32):
                    for c in range(3):
                        r0 = c * 4096
                        r1 = min(INTER, r0 + 4096)
                        jobs.append((nat, img, hb * 128, r0, r1))
            for ic in range(INTER // 128):
                jobs.append((d_nat, dT, ic * 128, 0, 4096))

            wrm = []
            for i, (nat, img, c0, r0, r1) in enumerate(jobs):
                sl = i % 2

                t_nm = "imt" if sl == 0 else "imu"
                g_nm = "imga" if sl == 0 else "imgb"

                @block.scalar
                def _(sc, nat=nat, c0=c0, r0=r0, r1=r1, sl=sl, i=i, t_nm=t_nm,
                      g_nm=g_nm):
                    if i == 0:
                        E.w(sc, "pre", n_pre_cast)
                    if i >= 2:
                        E.w(sc, g_nm, wrm[i - 2][1])
                    E.inc(t_nm, sc.dma_start(out=im_buf[:, sl, 0:r1 - r0],
                                             in_=nat.ap()[r0:r1, c0:c0 + 128],
                                             transpose=True), 16)
                m_t = E.n[t_nm]

                @block.sync
                def _(s, img=img, c0=c0, r0=r0, r1=r1, sl=sl, m_t=m_t, t_nm=t_nm,
                      g_nm=g_nm):
                    E.w(s, t_nm, m_t)
                    wrm.append((g_nm, E.inc(g_nm, s.dma_start(
                        out=img.ap()[c0:c0 + 128, r0:r1],
                        in_=im_buf[:, sl, 0:r1 - r0]), 16)))
            return (E.n["imga"], E.n["imgb"], E.n["imt"], E.n["imu"])

        def emit_ffn(m_imgs):
            f_x = A.bf(0, 2048).rearrange("p (sl s) -> p sl s", sl=2)
            f_w = A.bf(2048, 8192).rearrange("p (sl m b i) -> p sl m b i",
                                             sl=2, m=2, b=8)
            f_it = A.bf(10240, 4096).rearrange("p (ic s) -> p ic s", ic=4)
            f_sl = A.bf(14336, 1024)
            f_dw = A.bf(15360, 8192).rearrange("p (ic s) -> p ic s", ic=4)

            fe_mark = [E.n["vch"]]
            fs_mark = [E.n["vch"]]
            i0 = 0
            for ib, isz in enumerate(IB_SIZES):
                n_it = isz // 128
                for ig in range(n_it // 2):
                    ibase = i0 + ig * 256
                    wbm, xbm, gm = {}, {}, []
                    for hbB in range(4):
                        @block.sync
                        def _(s, ibase=ibase, hbB=hbB, ib=ib, ig=ig):
                            if ib == 0 and ig == 0 and hbB == 0:
                                E.w(s, "imga", m_imgs[0])
                                E.w(s, "imgb", m_imgs[1])
                                E.w(s, "imt", m_imgs[2])
                                E.w(s, "imu", m_imgs[3])
                                E.w(s, "lx", E.n["lx"])
                            if hbB >= 2:
                                E.w(s, "sp", gm[(hbB - 2) * 8 + 7])
                            elif ib + ig > 0:
                                E.w(s, "sp", E.n["sp"])
                            nm = "fwa" if hbB % 2 == 0 else "fwb"
                            E.inc(nm, s.dma_start(
                                out=f_w[:, hbB % 2, 0, :, :],
                                in_=gT.ap()[hbB * 1024:(hbB + 1) * 1024,
                                            ibase:ibase + 256]
                                    .rearrange("(b p) i -> p b i", p=128)), 16)

                        @block.scalar
                        def _(sc, ibase=ibase, hbB=hbB, ib=ib, ig=ig):
                            if ib == 0 and ig == 0 and hbB == 0:
                                E.w(sc, "imga", m_imgs[0])
                                E.w(sc, "imgb", m_imgs[1])
                                E.w(sc, "imt", m_imgs[2])
                                E.w(sc, "imu", m_imgs[3])
                                E.w(sc, "lx", E.n["lx"])
                            if hbB >= 2:
                                E.w(sc, "sp", gm[(hbB - 2) * 8 + 7])
                            elif ib + ig > 0:
                                E.w(sc, "sp", E.n["sp"])
                            nm = "fua" if hbB % 2 == 0 else "fub"
                            E.inc(nm, sc.dma_start(
                                out=f_w[:, hbB % 2, 1, :, :],
                                in_=uT.ap()[hbB * 1024:(hbB + 1) * 1024,
                                            ibase:ibase + 256]
                                    .rearrange("(b p) i -> p b i", p=128)), 16)
                        par2 = hbB % 2
                        wbm[hbB] = (("fwa" if par2 == 0 else "fwb",
                                     E.n["fwa" if par2 == 0 else "fwb"]),
                                    ("fua" if par2 == 0 else "fub",
                                     E.n["fua" if par2 == 0 else "fub"]))

                        for b in range(8):
                            hb = hbB * 8 + b

                            @block.sync
                            def _(s, hb=hb, ib=ib, ig=ig):
                                if hb >= 2:
                                    E.w(s, "sp", gm[hb - 2])
                                elif ib == 0 and ig == 0:
                                    E.w(s, "lx", E.n["lx"])
                                    E.w(s, "imga", m_imgs[0])
                                    E.w(s, "imgb", m_imgs[1])
                                    E.w(s, "imt", m_imgs[2])
                                    E.w(s, "imu", m_imgs[3])
                                else:
                                    E.w(s, "sp", E.n["sp"])
                                nm = "fxa" if hb % 2 == 0 else "fxb"
                                xbm[hb] = (nm, E.inc(nm, s.dma_start(
                                    out=f_x[:, hb % 2, :], in_=xt2.ap()[hb, :, :]), 16))

                            @block.tensor
                            def _(te, hb=hb, hbB=hbB, b=b, ib=ib, ig=ig):
                                E.w(te, xbm[hb][0], xbm[hb][1])
                                E.w(te, wbm[hbB][0][0], wbm[hbB][0][1])
                                E.w(te, wbm[hbB][1][0], wbm[hbB][1][1])
                                if hb == 0:
                                    if ig == 0:
                                        E.w(te, "vch", fe_mark[0])
                                    else:
                                        E.w(te, "vch", fs_mark[0])
                                last = None
                                for it2 in range(2):
                                    for mat in range(2):
                                        base = (it2 * 2 + mat) * 1024
                                        for half in range(2):
                                            last = te.matmul(
                                                pool[:, base + half * 512:
                                                     base + half * 512 + 512],
                                                f_w[:, hbB % 2, mat, b,
                                                    it2 * 128:(it2 + 1) * 128],
                                                f_x[:, hb % 2,
                                                    half * 512:half * 512 + 512],
                                                start=(hb == 0), stop=(hb == 31))
                                gm.append(E.inc("sp", last))

                    for it2 in range(2):
                        it = ig * 2 + it2

                        def f(sc, it2=it2, gm=gm):
                            E.w(sc._e, "sp", gm[31])
                            if it2 == 1:
                                E.w(sc._e, "vch", fs_mark[0])
                            sc.activation(f_sl, pool[:, it2 * 2048:it2 * 2048 + 1024],
                                          AF.Silu, scale=gsb[:, :])
                        BS(f)
                        m_sl = E.n["ach"]

                        def f(v, it=it, it2=it2, m_sl=m_sl):
                            E.w(v._e, "ach", m_sl)
                            E.w(v._e, "fd", E.n["fd"])
                            v.scalar_tensor_tensor(
                                f_it[:, it, :],
                                pool[:, it2 * 2048 + 1024:it2 * 2048 + 2048],
                                usb[:, :], f_sl, ALU.mult, ALU.mult)
                        BV(f)
                        fs_mark[0] = E.n["vch"]

                for half in range(2):
                    @block.sync
                    def _(s, i0=i0, n_it=n_it, half=half):
                        E.w(s, "fd", E.n["fd"])
                        for ic in range(n_it):
                            E.inc("dw", s.dma_start(
                                out=f_dw[:, ic, :],
                                in_=dT.ap()[i0 + ic * 128:i0 + (ic + 1) * 128,
                                            half * 2048:(half + 1) * 2048]), 16)
                    m_dw = E.n["dw"]

                    fdm = {}
                    dn_ev = {}
                    for st in range(8):
                        @block.tensor
                        def _(te, st=st, half=half, n_it=n_it, m_dw=m_dw):
                            E.w(te, "dw", m_dw)
                            if st == 0 and half == 0:
                                E.w(te, "vch", fs_mark[0])
                            if st >= 2:
                                E.w(te, "vch", dn_ev[st - 2])
                            elif half == 1 and st < 2:
                                E.w(te, "vch", fe_mark[0])
                            sl = (st % 2) * 2048
                            last = None
                            for ic in range(n_it):
                                for nn in range(4):
                                    last = te.matmul(
                                        pool[:, sl + nn * 512:sl + (nn + 1) * 512],
                                        f_it[:, ic, st * 128:(st + 1) * 128],
                                        f_dw[:, ic, nn * 512:(nn + 1) * 512],
                                        start=(ic == 0), stop=(ic == n_it - 1))
                            fdm[st] = E.inc("fd", last)

                        def f(v, st=st, half=half):
                            E.w(v._e, "fd", fdm[st])
                            sl = (st % 2) * 2048
                            v.scalar_tensor_tensor(
                                res[:, st, half * 2048:(half + 1) * 2048],
                                pool[:, sl:sl + 2048], dsb[:, :],
                                res[:, st, half * 2048:(half + 1) * 2048],
                                ALU.mult, ALU.add)
                        BV(f)
                        dn_ev[st] = E.n["vch"]
                    fe_mark[0] = E.n["vch"]
                i0 += isz

        KT_sb = A.bf(0, NKCL * 128)
        V_sb = A.bf(9216, NKCL * 128)
        lp_sb = A.bf(18432, NKCL * 128)
        rp_sb = A.bf(27648, 4096)

        def emit_attn(wq, wkv, wo):
            emit_ln()
            m_kv = emit_kv(wkv, KT_sb, V_sb)
            emit_qproj(wq, m_kv)
            barrier()
            emit_attn_main(KT_sb, V_sb, lp_sb, rp_sb, wo, m_kv)
            barrier()

        emit_attn(W["wq_a"], W["wkv_a"], W["wo_a"])
        m_imgs = emit_images()
        if n_sub >= 2:
            barrier()
            emit_ln()
            emit_ffn(m_imgs)
            barrier()
        if n_sub >= 3:
            emit_attn(W["wq_b"], W["wkv_b"], W["wo_b"])
        if n_sub >= 4:
            emit_ln()
            emit_ffn(m_imgs)
            barrier()

        @block.sync
        def _(s):
            E.w(s, "vch", E.n["vch"])
            E.w(s, "ach", E.n["ach"])
            E.w(s, "imga", E.n["imga"])
            E.w(s, "imgb", E.n["imgb"])
            E.w(s, "pre", E.n["pre"])
            E.inc("st", s.dma_start(out=out.rearrange("(t p) d -> p t d", p=128),
                                    in_=res[:, :, :]), 16)
            s.wait_ge(E.h["st"], E.n["st"])

    return nc


def _in_maps(inputs):
    xs = np.asarray(inputs["L0_attn_a_hidden_states"], np.float32)[0]
    base = {
        "wq_a": np.ascontiguousarray(np.asarray(inputs["L0_attn_a_Wq"], np.float32)),
        "wkv_a": np.ascontiguousarray(np.asarray(inputs["L0_attn_a_Wkv"], np.float32)),
        "wo_a": np.ascontiguousarray(np.asarray(inputs["L0_attn_a_Wo"], np.float32)),
        "wq_b": np.ascontiguousarray(np.asarray(inputs["L0_attn_b_Wq"], np.float32)),
        "wkv_b": np.ascontiguousarray(np.asarray(inputs["L0_attn_b_Wkv"], np.float32)),
        "wo_b": np.ascontiguousarray(np.asarray(inputs["L0_attn_b_Wo"], np.float32)),
        "gate_w": np.ascontiguousarray(np.asarray(inputs["L0_ffn_gate_w"], np.float32)),
        "up_w": np.ascontiguousarray(np.asarray(inputs["L0_ffn_up_w"], np.float32)),
        "down_w": np.ascontiguousarray(np.asarray(inputs["L0_ffn_down_w"], np.float32)),
        "gs_b": np.full((128, 1), np.float32(inputs["L0_ffn_gate_scale"]), np.float32),
        "us_b": np.full((128, 1), np.float32(inputs["L0_ffn_up_scale"]), np.float32),
        "ds_b": np.full((128, 1), np.float32(inputs["L0_ffn_down_scale"]), np.float32),
        "ident": np.eye(128, dtype=np.float32),
    }
    maps = []
    for c in range(NCORES):
        t = build_tables(c)
        m = dict(base)
        m["x"] = np.ascontiguousarray(xs[c * PER:(c + 1) * PER])
        m["t_lpack"] = t["lpack"]
        m["t_rpack"] = t["rpack"]
        m["t_expb"] = t["expb"]
        m["t_mask"] = t["mask"]
        maps.append(m)
    return maps


def kernel(**inputs):
    maps = _in_maps(inputs)
    nc = build_nc(NSUB, DUMP)
    res = run_bass_kernel_spmd(nc, maps, core_ids=list(range(NCORES)))
    outp = np.concatenate([res.results[c]["out"] for c in range(NCORES)], axis=0)
    return outp.reshape(1, SEQ, D)
